# version 02 (final)
# baseline (speedup 1.0000x reference)
import contextlib
import numpy as np
import ml_dtypes
import concourse.bass as bass
import concourse.mybir as mybir
from concourse.bass_utils import run_bass_kernel_spmd

F32 = mybir.dt.float32
BF16 = mybir.dt.bfloat16
I16 = mybir.dt.int16
ALU = mybir.AluOpType
AFT = mybir.ActivationFunctionType
AX = mybir.AxisListType

B, S, D = 4, 2048, 4096
NH, NKV, HD = 32, 8, 128
GRP = NH // NKV
HALF = 1024
T = 512
NTILE = 2
NCH = T // 128
KD = NKV * HD
NQH = 16
NKVH = 4
MIXD = NQH * HD
WSZ = 64
E, R = 8, 16
EPS = 1e-5
SCALE_A = HD ** -0.5
SCALE_B = HD ** -0.5
NEGINF = -1e30
NCORE = 8
RG = [[0, 1], [2, 3], [4, 5], [6, 7]]

ENGS = ("sync", "gpsimd", "vector", "scalar", "tensor")


class Ctr:
    def __init__(self):
        self.sem = None
        self.v = 0


class Prog:
    def __init__(self):
        self.steps = {e: [] for e in ENGS}
        self.nbar = 0

    def emit(self, eng, build, waits=(), inc=None, inc_n=1):
        best = {}
        for (c, v) in waits:
            if v > best.get(id(c), (None, -1))[1]:
                best[id(c)] = (c, v)
        wl = list(best.values())
        if inc is not None:
            inc.v += inc_n

        def run(h, _b=build, _w=wl, _i=inc, _n=inc_n):
            for c, v in _w:
                if v > 0:
                    h.wait_ge(c.sem, v)
            inst = _b(h)
            if _i is not None:
                inst.then_inc(_i.sem, _n)
        self.steps[eng].append(run)

    def wait(self, eng, ctr, val=None):
        v = ctr.v if val is None else val
        if v > 0:
            self.steps[eng].append(lambda h, _c=ctr, _v=v: h.wait_ge(_c.sem, _v))

    def barrier(self, bar):
        self.nbar += 1
        tgt = 5 * self.nbar
        for e in ENGS:
            self.steps[e].append(lambda h, _c=bar: h.sem_inc(_c.sem, 1))
            self.steps[e].append(lambda h, _c=bar, _v=tgt: h.wait_ge(_c.sem, _v))
        bar.v = tgt


def build_program(dbg=False):
    nc = bass.Bass("TRN2", target_bir_lowering=False)

    xin = nc.dram_tensor("xin", [HALF, D], F32, kind="ExternalInput").ap()
    waq = nc.dram_tensor("waq", [D, D], F32, kind="ExternalInput").ap()
    wak = nc.dram_tensor("wak", [D, KD], F32, kind="ExternalInput").ap()
    wav = nc.dram_tensor("wav", [D, KD], F32, kind="ExternalInput").ap()
    wao = nc.dram_tensor("wao", [D, D], F32, kind="ExternalInput").ap()
    wbq = nc.dram_tensor("wbq", [D, D], F32, kind="ExternalInput").ap()
    wbk = nc.dram_tensor("wbk", [D, KD], F32, kind="ExternalInput").ap()
    wbv = nc.dram_tensor("wbv", [D, KD], F32, kind="ExternalInput").ap()
    wbo = nc.dram_tensor("wbo", [D, D], F32, kind="ExternalInput").ap()
    rwT = nc.dram_tensor("rwT", [D, E], F32, kind="ExternalInput").ap()
    a2T = nc.dram_tensor("a2T", [D, E * R], F32, kind="ExternalInput").ap()
    bw2 = nc.dram_tensor("bw2", [E * R, D], F32, kind="ExternalInput").ap()
    identF = nc.dram_tensor("identF", [128, 128], F32, kind="ExternalInput").ap()
    maskAin = nc.dram_tensor("maskAin", [8, 128, S], F32, kind="ExternalInput").ap()
    maskSWAin = nc.dram_tensor("maskSWAin", [2, 64, 128], F32, kind="ExternalInput").ap()
    maskGLAin = nc.dram_tensor("maskGLAin", [128, 128], F32, kind="ExternalInput").ap()
    costab = nc.dram_tensor("costab", [HD, S], F32, kind="ExternalInput").ap()
    sintab = nc.dram_tensor("sintab", [HD, S], F32, kind="ExternalInput").ap()
    selin = nc.dram_tensor("selin", [128, 2], F32, kind="ExternalInput").ap()
    yout = nc.dram_tensor("yout", [HALF, D], F32, kind="ExternalOutput").ap()
    dbgo = {}
    if dbg:
        dbgo["dx1"] = nc.dram_tensor("dx1", [HALF, D], F32, kind="ExternalOutput").ap()
        dbgo["dx2"] = nc.dram_tensor("dx2", [HALF, D], F32, kind="ExternalOutput").ap()
        dbgo["dx3"] = nc.dram_tensor("dx3", [HALF, D], F32, kind="ExternalOutput").ap()
        dbgo["dqb"] = nc.dram_tensor("dqb", [HALF, D], F32, kind="ExternalOutput").ap()
        dbgo["dkb"] = nc.dram_tensor("dkb", [HALF, KD], F32, kind="ExternalOutput").ap()
        dbgo["dvb"] = nc.dram_tensor("dvb", [HALF, KD], F32, kind="ExternalOutput").ap()
        dbgo["domix"] = nc.dram_tensor("domix", [MIXD, S], BF16, kind="ExternalOutput").ap()
        dbgo["dstate"] = nc.dram_tensor("dstate", [16, 128, 128], F32, kind="ExternalOutput").ap()
        dbgo["dA"] = nc.dram_tensor("dA", [16, 128, 128], F32, kind="ExternalOutput").ap()
        dbgo["dK3"] = nc.dram_tensor("dK3", [16, 128, 128], F32, kind="ExternalOutput").ap()
        dbgo["dDl"] = nc.dram_tensor("dDl", [16, 128, 1], F32, kind="ExternalOutput").ap()
        dbgo["domA"] = nc.dram_tensor("domA", [128, 32 * T], F32, kind="ExternalOutput").ap()

    cc1k_in = nc.dram_tensor("cc1k_in", [KD, HALF], F32, kind="Internal").ap()
    cc1v_in = nc.dram_tensor("cc1v_in", [HALF, KD], F32, kind="Internal").ap()
    cc1k_out = nc.dram_tensor("cc1k_out", [2 * KD, HALF], F32, kind="Internal").ap()
    cc1v_out = nc.dram_tensor("cc1v_out", [2 * HALF, KD], F32, kind="Internal").ap()
    cc2q_in = nc.dram_tensor("cc2q_in", [HALF, D], F32, kind="Internal").ap()
    cc2k_in = nc.dram_tensor("cc2k_in", [HALF, KD], F32, kind="Internal").ap()
    cc2v_in = nc.dram_tensor("cc2v_in", [HALF, KD], F32, kind="Internal").ap()
    cc2q_out = nc.dram_tensor("cc2q_out", [2 * HALF, D], F32, kind="Internal").ap()
    cc2k_out = nc.dram_tensor("cc2k_out", [2 * HALF, KD], F32, kind="Internal").ap()
    cc2v_out = nc.dram_tensor("cc2v_out", [2 * HALF, KD], F32, kind="Internal").ap()
    cc3_in_t = nc.dram_tensor("cc3_in", [MIXD, S], BF16, kind="Internal")
    cc3_in = cc3_in_t.ap()
    cc3_out = nc.dram_tensor("cc3_out", [2 * MIXD, S], BF16, kind="Internal").ap()
    x2sp = nc.dram_tensor("x2sp", [HALF, D], F32, kind="Internal").ap()
    qa_sp = nc.dram_tensor("qa_sp", [D, T], F32, kind="Internal").ap()

    P = Prog()
    sd, sg, s2, tt, tv, ts, cc, bar = (Ctr() for _ in range(8))
    ctrs = {"sd": sd, "sg": sg, "s2": s2, "tt": tt, "tv": tv, "ts": ts, "cc": cc, "bar": bar}

    stack = contextlib.ExitStack()
    for name, c in ctrs.items():
        c.sem = stack.enter_context(nc.semaphore(name))

    def sbuf(name, p, f, dt=F32):
        return stack.enter_context(nc.sbuf_tensor(name, [p, f], dt))

    x_t = sbuf("x_t", 128, NCH * D)
    arA = sbuf("arA", 128, 32 * T)
    arB = sbuf("arB", 128, 8192)
    ws0 = sbuf("ws0", 128, 1024)
    ws1 = sbuf("ws1", 128, 1024)
    small = sbuf("small", 128, 2048)
    ident = sbuf("ident", 128, 128)
    mgla = sbuf("mgla", 128, 128)
    mswa = sbuf("mswa", 64, 256)
    sel = sbuf("sel", 128, 2)
    stat = sbuf("stat", 128, 32)
    sgla = sbuf("sgla", 128, NKVH * 128)
    rollk = sbuf("rollk", 128, NKVH * WSZ)
    rollv = sbuf("rollv", 64, NKVH * 128)
    cosb = sbuf("cosb", 128, S)
    sinb = sbuf("sinb", 128, S)
    obf = sbuf("obf", 128, NQH * 128, I16)
    rwbuf = sbuf("rwbuf", 128, 256)
    zbuf = sbuf("zbuf", 128, 128)
    smb = sbuf("smb", 128, 256, BF16)
    mixw = arA
    pb = [stack.enter_context(nc.psum_tensor(f"pb{i}", [128, 512], F32)) for i in range(8)]

    def dma(eng, ctr, out, in_, waits=()):
        P.emit(eng, lambda h, _o=out, _i=in_: h.dma_start(out=_o, in_=_i),
               waits=list(waits) + [(ctr, ctr.v)], inc=ctr, inc_n=16)

    def tens(build, waits=(), n=1):
        P.emit("tensor", build, waits=list(waits), inc=tt, inc_n=n)

    def vec(build, waits=(), n=1):
        P.emit("vector", build, waits=[(tv, tv.v), (ts, ts.v)] + list(waits), inc=tv, inc_n=n)

    def sca(build, waits=(), n=1):
        P.emit("scalar", build, waits=[(ts, ts.v), (tv, tv.v)] + list(waits), inc=ts, inc_n=n)

    dma("sync", sd, ident[:], identF)
    dma("sync", sd, mgla[:], maskGLAin)
    dma("sync", sd, mswa[:, 0:128], maskSWAin[0])
    dma("sync", sd, mswa[:, 128:256], maskSWAin[1])
    dma("sync", sd, sel[:], selin)
    dma("sync", sd, cosb[:], costab)
    dma("sync", sd, sinb[:], sintab)
    for k in range(32):
        dma("scalar", s2, rwbuf[:, k * 8:(k + 1) * 8], rwT[k * 128:(k + 1) * 128, :])
    vec(lambda h: h.memset(sgla[:], 0.0))
    vec(lambda h: h.memset(rollk[:], 0.0))
    vec(lambda h: h.memset(rollv[:], 0.0))
    vec(lambda h: h.memset(zbuf[:], 0.0))
    P.barrier(bar)

    def load_xtile(tile, src=xin):
        for t in range(NCH):
            dma("sync", sd, x_t[:, t * D:(t + 1) * D],
                src[tile * T + t * 128: tile * T + (t + 1) * 128, :],
                waits=[(tv, tv.v), (tt, tt.v), (ts, ts.v), (sg, sg.v), (sd, sd.v)])

    def emit_ln():
        for t in range(NCH):
            xt = x_t[:, t * D:(t + 1) * D]
            scr = arB[:, (t % 2) * D:(t % 2) * D + D]
            c0 = t * 8
            vec(lambda h, _x=xt, _s=scr, _a=stat[:, c0:c0 + 1]:
                h.scalar_tensor_tensor(_s, _x, 1.0, _x, op0=ALU.mult, op1=ALU.mult, accum_out=_a),
                waits=[(tt, tt.v), (sd, sd.v), (sg, sg.v)])
            vec(lambda h, _x=xt, _o=stat[:, c0 + 1:c0 + 2]: h.reduce_sum(_o, _x, axis=AX.X))
            sca(lambda h, _i=stat[:, c0 + 1:c0 + 2], _o=stat[:, c0 + 2:c0 + 3]:
                h.activation(_o, _i, AFT.Copy, scale=1.0 / D), waits=[(tv, tv.v)])
            sca(lambda h, _i=stat[:, c0 + 2:c0 + 3], _o=stat[:, c0 + 3:c0 + 4]:
                h.activation(_o, _i, AFT.Square))
            vec(lambda h, _i=stat[:, c0:c0 + 1], _m=stat[:, c0 + 3:c0 + 4], _o=stat[:, c0 + 4:c0 + 5]:
                h.scalar_tensor_tensor(_o, _i, 1.0 / D, _m, op0=ALU.mult, op1=ALU.subtract),
                waits=[(ts, ts.v)])
            vec(lambda h, _i=stat[:, c0 + 4:c0 + 5]:
                h.tensor_scalar(_i, _i, EPS, None, op0=ALU.add))
            sca(lambda h, _i=stat[:, c0 + 4:c0 + 5], _o=stat[:, c0 + 5:c0 + 6]:
                h.activation(_o, _i, AFT.Sqrt), waits=[(tv, tv.v)])
            vec(lambda h, _i=stat[:, c0 + 5:c0 + 6]: h.reciprocal(_i, _i), waits=[(ts, ts.v)])
            vec(lambda h, _m=stat[:, c0 + 2:c0 + 3], _r=stat[:, c0 + 5:c0 + 6], _o=stat[:, c0 + 6:c0 + 7]:
                h.tensor_tensor(_o, _m, _r, op=ALU.mult))
            vec(lambda h, _i=stat[:, c0 + 6:c0 + 7], _o=stat[:, c0 + 7:c0 + 8]:
                h.tensor_scalar(_o, _i, -1.0, None, op0=ALU.mult))
            vec(lambda h, _x=xt, _s=scr, _r=stat[:, c0 + 5:c0 + 6], _b=stat[:, c0 + 7:c0 + 8]:
                h.tensor_scalar(_s, _x, _r, _b, op0=ALU.mult, op1=ALU.add))
            vready = tv.v
            for k in range(32):
                bk = pb[k % 4]
                tens(lambda h, _s=scr[:, k * 128:(k + 1) * 128], _p=bk[:, 0:128]:
                     h.transpose(_p, _s, ident[:]),
                     waits=[(tv, vready if k < 4 else tv.v)])
                vec(lambda h, _p=bk[:, 0:128], _d=arA[:, k * T + t * 128: k * T + (t + 1) * 128]:
                    h.tensor_copy(_d, _p), waits=[(tt, tt.v)])

    ws_last = {0: 0, 1: 0}

    def proj_fmajor(wsrc, mout, spill_to):
        last_use = ws_last
        for mg in range(mout // 1024):
            for k in range(32):
                w = ws0 if (k % 2 == 0) else ws1
                dma("sync", sd, w[:], wsrc[k * 128:(k + 1) * 128, mg * 1024:(mg + 1) * 1024],
                    waits=[(tt, last_use[k % 2])])
                P.wait("tensor", sd)
                for m in range(8):
                    tens(lambda h, _p=pb[m][:], _w=w[:, m * 128:(m + 1) * 128],
                         _x=arA[:, k * T:(k + 1) * T], _s=(k == 0), _e=(k == 31):
                         h.matmul(_p, _w, _x, start=_s, stop=_e),
                         waits=([(tv, tv.v)] if k == 0 else []))
                last_use[k % 2] = tt.v
            for m in range(8):
                vec(lambda h, _p=pb[m][:], _d=small[:, (m % 2) * 512:(m % 2) * 512 + 512]:
                    h.tensor_copy(_d, _p), waits=[(tt, tt.v), (sg, sg.v)])
                dma("gpsimd", sg, spill_to(mg * 8 + m), small[:, (m % 2) * 512:(m % 2) * 512 + 512],
                    waits=[(tv, tv.v)])

    def proj_tokmajor(wsrc, nout, consume, stationary, kchunks=32):
        last_use = ws_last
        for dg in range(nout // 512):
            for k in range(kchunks):
                w = ws0 if (k % 2 == 0) else ws1
                dma("sync", sd, w[:, 0:512], wsrc[k * 128:(k + 1) * 128, dg * 512:(dg + 1) * 512],
                    waits=[(tt, last_use[k % 2])])
                P.wait("tensor", sd)
                for t in range(NCH):
                    tens(lambda h, _p=pb[4 + t][:], _l=stationary(k, t), _w=w[:, 0:512],
                         _s=(k == 0), _e=(k == kchunks - 1):
                         h.matmul(_p, _l, _w, start=_s, stop=_e),
                         waits=([(tv, tv.v)] if k == 0 else []))
                last_use[k % 2] = tt.v
            for t in range(NCH):
                consume(dg, t, pb[4 + t][:])

    def xnf_stat(k, t):
        return arA[:, k * T + t * 128: k * T + (t + 1) * 128]

    for tile in range(NTILE):
        load_xtile(tile)
        P.barrier(bar)
        emit_ln()
        P.barrier(bar)
        proj_fmajor(wak, KD,
                    lambda mi, _t=tile: cc1k_in[mi * 128:(mi + 1) * 128, _t * T:(_t + 1) * T])

        def cons_v(dg, t, p, _tile=tile):
            vec(lambda h, _p=p, _d=small[:, 1024:1536]: h.tensor_copy(_d, _p),
                waits=[(tt, tt.v), (sg, sg.v)])
            dma("gpsimd", sg,
                cc1v_in[_tile * T + t * 128: _tile * T + (t + 1) * 128, dg * 512:(dg + 1) * 512],
                small[:, 1024:1536], waits=[(tv, tv.v)])
        proj_tokmajor(wav, KD, cons_v, xnf_stat)
        P.barrier(bar)

    P.emit("gpsimd", lambda h: h.collective_compute(
        "AllGather", ALU.bypass, ins=[cc1k_in], outs=[cc1k_out], replica_groups=RG),
        waits=[(sg, sg.v)], inc=cc)
    P.emit("gpsimd", lambda h: h.collective_compute(
        "AllGather", ALU.bypass, ins=[cc1v_in], outs=[cc1v_out], replica_groups=RG), inc=cc)
    cc1_done = cc.v

    def attn_a(tile):
        proj_fmajor(waq, D, lambda mi: qa_sp[mi * 128:(mi + 1) * 128, :])
        P.barrier(bar)
        KE = 1024 + 512 * (tile + 1)
        kw = arB[:, 0:2048]
        vw = arB[:, 2048:4096]
        srow = arB[:, 4096:6144]
        mrow = arB[:, 6144:8192]
        P.wait("sync", cc, cc1_done)
        P.wait("gpsimd", cc, cc1_done)
        P.wait("scalar", cc, cc1_done)
        for kvh in range(NKV):
            for r in range(2):
                lo, hi = r * 1024, min(KE, (r + 1) * 1024)
                if lo >= KE:
                    break
                dma("sync", sd, kw[:, lo:hi],
                    cc1k_out[r * KD + kvh * 128: r * KD + (kvh + 1) * 128, 0:hi - lo],
                    waits=[(tt, tt.v)])
            for pch in range(KE // 128):
                dma("gpsimd", sg, vw[:, pch * 128:(pch + 1) * 128],
                    cc1v_out[pch * 128:(pch + 1) * 128, kvh * 128:(kvh + 1) * 128],
                    waits=[(tt, tt.v)])
            for hh in range(GRP):
                h_q = kvh * GRP + hh
                for qt in range(NCH):
                    qi = tile * NCH + qt
                    dma("scalar", s2, small[:, 1024:1152],
                        qa_sp[h_q * 128:(h_q + 1) * 128, qt * 128:(qt + 1) * 128],
                        waits=[(tt, tt.v), (sg, sg.v)])
                    dma("scalar", s2, mrow[:, 0:KE], maskAin[qi, :, 0:KE],
                        waits=[(tv, tv.v)])
                    for kc in range(KE // 512):
                        tens(lambda h, _p=pb[kc % 2][:], _q=small[:, 1024:1152],
                             _k=kw[:, kc * 512:(kc + 1) * 512]:
                             h.matmul(_p, _q, _k, start=True, stop=True),
                             waits=[(s2, s2.v), (sd, sd.v), (sg, sg.v), (tv, tv.v)])
                        vec(lambda h, _p=pb[kc % 2][:], _d=srow[:, kc * 512:(kc + 1) * 512],
                            _m=mrow[:, kc * 512:(kc + 1) * 512]:
                            h.scalar_tensor_tensor(_d, _p, SCALE_A, _m, op0=ALU.mult, op1=ALU.add),
                            waits=[(tt, tt.v), (s2, s2.v)])
                    vec(lambda h, _s=srow[:, 0:KE]:
                        h.reduce_max(stat[:, 8:9], _s, axis=AX.X, negate=True))
                    sca(lambda h, _s=srow[:, 0:KE]:
                        h.activation(_s, _s, AFT.Exp, bias=stat[:, 8:9], accum_out=stat[:, 9:10]),
                        waits=[(tv, tv.v)])
                    vec(lambda h: h.reciprocal(stat[:, 10:11], stat[:, 9:10]), waits=[(ts, ts.v)])
                    vec(lambda h, _s=srow[:, 0:KE]: h.tensor_scalar_mul(_s, _s, stat[:, 10:11]))
                    softdone = tv.v
                    nk = KE // 128
                    ngrp = (nk + 3) // 4
                    evac_tv = {}
                    for g in range(ngrp):
                        kcs = range(g * 4, min((g + 1) * 4, nk))
                        bank = pb[2 + (g % 2)]
                        for kc in kcs:
                            tens(lambda h, _p=bank[:, (kc % 4) * 128:(kc % 4) * 128 + 128],
                                 _s=srow[:, kc * 128:(kc + 1) * 128],
                                 _st=(kc == kcs[0]), _en=(kc == kcs[-1]):
                                 h.matmul(_p, _s, ident[:], start=_st, stop=_en,
                                          is_transpose=True),
                                 waits=[(tv, max(softdone, evac_tv.get(g - 2, 0)))])
                        vec(lambda h, _p=bank[:, 0:len(kcs) * 128],
                            _d=small[:, (g % 2) * 512:(g % 2) * 512 + len(kcs) * 128]:
                            h.tensor_copy(_d, _p), waits=[(tt, tt.v)])
                        evac_tv[g] = tv.v
                        for kc in kcs:
                            tens(lambda h, _p=pb[7][:, 0:128], _v=vw[:, kc * 128:(kc + 1) * 128],
                                 _pt=small[:, (g % 2) * 512 + (kc % 4) * 128:
                                           (g % 2) * 512 + (kc % 4) * 128 + 128],
                                 _s=(kc == 0), _e=(kc == nk - 1):
                                 h.matmul(_p, _v, _pt, start=_s, stop=_e),
                                 waits=[(tv, evac_tv[g])])
                    vec(lambda h, _p=pb[7][:, 0:128],
                        _d=arA[:, h_q * T + qt * 128: h_q * T + (qt + 1) * 128]:
                        h.tensor_copy(_d, _p), waits=[(tt, tt.v)])
        P.barrier(bar)

        def cons_wo(dg, t, p):
            vec(lambda h, _p=p, _x=x_t[:, t * D + dg * 512: t * D + (dg + 1) * 512]:
                h.tensor_tensor(_x, _x, _p, op=ALU.add), waits=[(tt, tt.v)])
        proj_tokmajor(wao, D, cons_wo, xnf_stat)
        P.barrier(bar)

    def emit_ffn():
        for k in range(32):
            dma("sync", sd, arB[:, k * 128:(k + 1) * 128], a2T[k * 128:(k + 1) * 128, :],
                waits=[(tv, tv.v), (tt, tt.v)] if k == 0 else ())
        for k in range(32):
            dma("sync", sd, arB[:, 4096 + k * 128:4096 + (k + 1) * 128],
                bw2[:, k * 128:(k + 1) * 128])
        P.wait("tensor", sd)
        for t in range(NCH):
            for k in range(32):
                tens(lambda h, _p=pb[0][:, 0:8], _l=xnf_stat(k, t),
                     _w=rwbuf[:, k * 8:(k + 1) * 8], _s=(k == 0), _e=(k == 31):
                     h.matmul(_p, _l, _w, start=_s, stop=_e),
                     waits=[(tv, tv.v)] if k == 0 else ())
            vec(lambda h: h.tensor_copy(small[:, 0:8], pb[0][:, 0:8]), waits=[(tt, tt.v)])
            vec(lambda h: h.reduce_max(stat[:, 12:13], small[:, 0:8], axis=AX.X))
            vec(lambda h: h.tensor_scalar(small[:, 8:16], small[:, 0:8], stat[:, 12:13], None,
                                          op0=ALU.is_equal))
            vec(lambda h: h.tensor_tensor(small[:, 0:8], small[:, 0:8], small[:, 8:16],
                                          op=ALU.mult))
            for k in range(32):
                tens(lambda h, _p=pb[1][:, 0:128], _l=xnf_stat(k, t),
                     _w=arB[:, k * 128:(k + 1) * 128], _s=(k == 0), _e=(k == 31):
                     h.matmul(_p, _l, _w, start=_s, stop=_e),
                     waits=([(tv, tv.v)] if k == 0 else []))
            for e in range(E):
                vec(lambda h, _e=e: h.tensor_scalar_mul(
                    small[:, 1024 + _e * 16:1024 + (_e + 1) * 16],
                    pb[1][:, _e * 16:(_e + 1) * 16], small[:, _e:_e + 1]),
                    waits=[(tt, tt.v)] if e == 0 else ())
            tens(lambda h: h.transpose(pb[2][:, 0:128], small[:, 1024:1152], ident[:]),
                 waits=[(tv, tv.v)])
            vec(lambda h: h.tensor_copy(small[:, 1152:1280], pb[2][:, 0:128]),
                waits=[(tt, tt.v)])
            for dg in range(8):
                tens(lambda h, _p=pb[3][:], _l=small[:, 1152:1280],
                     _w=arB[:, 4096 + dg * 512:4096 + (dg + 1) * 512]:
                     h.matmul(_p, _l, _w, start=True, stop=True),
                     waits=[(tv, tv.v)])
                vec(lambda h, _p=pb[3][:], _x=x_t[:, t * D + dg * 512: t * D + (dg + 1) * 512]:
                    h.tensor_tensor(_x, _x, _p, op=ALU.add), waits=[(tt, tt.v)])

    for tile in range(NTILE):
        load_xtile(tile)
        P.barrier(bar)
        emit_ln()
        P.barrier(bar)
        attn_a(tile)
        if dbg:
            for t in range(NCH):
                dma("gpsimd", sg, dbgo["dx1"][tile * T + t * 128: tile * T + (t + 1) * 128, :],
                    x_t[:, t * D:(t + 1) * D], waits=[(tv, tv.v)])
        P.barrier(bar)
        emit_ln()
        P.barrier(bar)
        emit_ffn()
        P.barrier(bar)
        if dbg:
            for t in range(NCH):
                dma("gpsimd", sg, dbgo["dx2"][tile * T + t * 128: tile * T + (t + 1) * 128, :],
                    x_t[:, t * D:(t + 1) * D], waits=[(tv, tv.v)])
        emit_ln()
        P.barrier(bar)
        for wsrc, nout, dst in ((wbq, D, cc2q_in), (wbk, KD, cc2k_in), (wbv, KD, cc2v_in)):
            def cons(dg, t, p, _dst=dst, _tile=tile):
                vec(lambda h, _p=p, _d=small[:, 1024:1536]: h.tensor_copy(_d, _p),
                    waits=[(tt, tt.v), (sg, sg.v)])
                dma("gpsimd", sg,
                    _dst[_tile * T + t * 128: _tile * T + (t + 1) * 128, dg * 512:(dg + 1) * 512],
                    small[:, 1024:1536], waits=[(tv, tv.v)])
            proj_tokmajor(wsrc, nout, cons, xnf_stat)
        for t in range(NCH):
            dma("gpsimd", sg, x2sp[tile * T + t * 128: tile * T + (t + 1) * 128, :],
                x_t[:, t * D:(t + 1) * D], waits=[(tv, tv.v)])
        P.barrier(bar)

    P.emit("gpsimd", lambda h: h.collective_compute(
        "AllGather", ALU.bypass, ins=[cc2q_in], outs=[cc2q_out], replica_groups=RG),
        waits=[(sg, sg.v)], inc=cc)
    P.emit("gpsimd", lambda h: h.collective_compute(
        "AllGather", ALU.bypass, ins=[cc2k_in], outs=[cc2k_out], replica_groups=RG), inc=cc)
    P.emit("gpsimd", lambda h: h.collective_compute(
        "AllGather", ALU.bypass, ins=[cc2v_in], outs=[cc2v_out], replica_groups=RG), inc=cc)
    cc2_done = cc.v
    P.wait("sync", cc, cc2_done)
    P.wait("scalar", cc, cc2_done)
    P.wait("vector", cc, cc2_done)
    P.wait("tensor", cc, cc2_done)
    P.barrier(bar)

    def mslot(kvh, i):
        base = (kvh * 10 + i) * 128
        return mixw[:, base:base + 128]

    ktok = arB[:, 0:512]
    vtok = arB[:, 512:1024]
    cand0 = arB[:, 1024:1536]
    cand1 = arB[:, 1536:2048]
    qtok = arB[:, 2048:2176]
    qs_t = arB[:, 2176:2304]
    qf = arB[:, 2304:2432]
    qsf = arB[:, 2432:2560]
    qtl = arB[:, 2560:2688]
    sqf = arB[:, 2688:2816]
    pP = arB[:, 2816:2944]
    pswT = arB[:, 2944:3072]
    swin = arB[:, 3072:3200]
    svwin = arB[:, 3200:3328]
    sw = arB[:, 3328:3456]
    tmpm = arB[:, 3456:3584]
    kc_ = arB[:, 3584:3712]
    ks2_ = arB[:, 3712:3840]
    rotb = arB[:, 3840:3968]
    oglab = arB[:, 3968:4096]

    for c in range(16):
        r = c // 8
        for (dstc, src) in ((ktok, cc2k_out), (vtok, cc2v_out)):
            dma("sync", sd, cand0[:], src[c * 128:(c + 1) * 128, 0:512],
                waits=[(tv, tv.v), (tt, tt.v)])
            dma("sync", sd, cand1[:], src[c * 128:(c + 1) * 128, 512:1024])
            P.wait("vector", sd)
            vec(lambda h, _c=cand0, _r=r, _d=dstc:
                h.tensor_scalar_mul(_d, _c, sel[:, _r:_r + 1]))
            vec(lambda h, _c=cand1, _r=r, _d=dstc:
                h.scalar_tensor_tensor(_d, _c, sel[:, 1 - _r:2 - _r], _d, op0=ALU.mult, op1=ALU.add))
        selkv = tv.v

        for kvh in range(NKVH):
            kslice = ktok[:, kvh * 128:(kvh + 1) * 128]
            vslice = vtok[:, kvh * 128:(kvh + 1) * 128]
            tens(lambda h, _p=pb[0][:, 0:128], _k=kslice: h.transpose(_p, _k, ident[:]),
                 waits=[(tv, selkv if kvh == 0 else tv.v)])
            vec(lambda h, _p=pb[0][:, 0:128], _d=mslot(kvh, 0): h.tensor_copy(_d, _p),
                waits=[(tt, tt.v)])
            vec(lambda h, _k=kslice: h.reduce_max(stat[:, 14:15], _k, axis=AX.X, negate=True))
            sca(lambda h, _k=kslice, _o=qtl:
                h.activation(_o, _k, AFT.Exp, bias=stat[:, 14:15], accum_out=stat[:, 15:16]),
                waits=[(tv, tv.v)])
            vec(lambda h: h.reciprocal(stat[:, 16:17], stat[:, 15:16]), waits=[(ts, ts.v)])
            vec(lambda h, _o=qtl: h.tensor_scalar_mul(_o, _o, stat[:, 16:17]))
            tens(lambda h, _p=pb[1][:, 0:128], _o=qtl: h.transpose(_p, _o, ident[:]),
                 waits=[(tv, tv.v)])
            vec(lambda h, _p=pb[1][:, 0:128], _d=mslot(kvh, 1): h.tensor_copy(_d, _p),
                waits=[(tt, tt.v)])
            sca(lambda h, _k=mslot(kvh, 0), _o=mslot(kvh, 7):
                h.activation(_o, _k, AFT.Exp, scale=-1.0), waits=[(tv, tv.v)])
            sca(lambda h, _i=mslot(kvh, 7), _o=mslot(kvh, 7):
                h.activation(_o, _i, AFT.Ln, bias=1.0))
            vec(lambda h, _l=mslot(kvh, 7), _a=mslot(kvh, 2):
                h.tensor_tensor_scan(_a, _l, zbuf[:], 0.0, op0=ALU.add, op1=ALU.add),
                waits=[(ts, ts.v)])
            sca(lambda h, _a=mslot(kvh, 2), _o=mslot(kvh, 3):
                h.activation(_o, _a, AFT.Exp, scale=-1.0 / 16.0), waits=[(tv, tv.v)])
            sca(lambda h, _a=mslot(kvh, 2), _o=mslot(kvh, 7):
                h.activation(_o, _a, AFT.Exp, scale=1.0 / 16.0))
            vec(lambda h, _k=mslot(kvh, 1), _e=mslot(kvh, 7), _o=mslot(kvh, 4):
                h.tensor_tensor(_o, _k, _e, op=ALU.mult), waits=[(ts, ts.v)])
            sca(lambda h, _a=mslot(kvh, 2): h.activation(stat[:, 17:18], _a[:, 127:128],
                                                         AFT.Copy, scale=-1.0 / 16.0))
            sca(lambda h, _a=mslot(kvh, 2), _o=stat[:, 18 + kvh:19 + kvh]:
                h.activation(_o, _a[:, 127:128], AFT.Exp, scale=-1.0 / 16.0))
            sca(lambda h, _a=mslot(kvh, 2), _o=mslot(kvh, 7):
                h.activation(_o, _a, AFT.Exp, bias=stat[:, 17:18], scale=1.0 / 16.0))
            vec(lambda h, _k=mslot(kvh, 1), _e=mslot(kvh, 7), _o=mslot(kvh, 7):
                h.tensor_tensor(_o, _k, _e, op=ALU.mult), waits=[(ts, ts.v)])
            tens(lambda h, _p=pb[2][:, 0:128], _i=mslot(kvh, 7): h.transpose(_p, _i, ident[:]),
                 waits=[(tv, tv.v)])
            vec(lambda h, _p=pb[2][:, 0:128], _d=mslot(kvh, 5): h.tensor_copy(_d, _p),
                waits=[(tt, tt.v)])
            sca(lambda h, _k=mslot(kvh, 0): h.activation(rotb[0:64, :], _k[64:128, :],
                                                         AFT.Copy, scale=-1.0))
            sca(lambda h, _k=mslot(kvh, 0): h.activation(rotb[64:128, :], _k[0:64, :], AFT.Copy))
            vec(lambda h, _k=mslot(kvh, 0), _c=cosb[:, c * 128:(c + 1) * 128], _o=kc_:
                h.tensor_tensor(_o, _k, _c, op=ALU.mult), waits=[(ts, ts.v)])
            vec(lambda h, _s=sinb[:, c * 128:(c + 1) * 128], _o=ks2_:
                h.tensor_tensor(_o, rotb, _s, op=ALU.mult))
            vec(lambda h, _o=mslot(kvh, 6): h.tensor_tensor(_o, kc_, ks2_, op=ALU.add))

        prep_done_v, prep_done_t, prep_done_s = tv.v, tt.v, ts.v

        for qh in range(NQH):
            kvh = qh // GRP
            dma("scalar", s2, qtok, cc2q_out[c * 128:(c + 1) * 128, qh * 128:(qh + 1) * 128],
                waits=[(tv, tv.v)])
            dma("scalar", s2, qtl, cc2q_out[c * 128:(c + 1) * 128, MIXD + qh * 128:MIXD + (qh + 1) * 128])
            P.wait("vector", s2)
            vec(lambda h, _r=r: h.tensor_scalar_mul(qtok, qtok, sel[:, _r:_r + 1]))
            vec(lambda h, _r=r: h.scalar_tensor_tensor(qtok, qtl, sel[:, 1 - _r:2 - _r], qtok,
                                                       op0=ALU.mult, op1=ALU.add))
            vec(lambda h: h.reduce_max(stat[:, 22:23], qtok, axis=AX.X, negate=True))
            sca(lambda h: h.activation(qs_t, qtok, AFT.Exp, bias=stat[:, 22:23],
                                       accum_out=stat[:, 23:24]), waits=[(tv, tv.v)])
            vec(lambda h: h.reciprocal(stat[:, 24:25], stat[:, 23:24]), waits=[(ts, ts.v)])
            vec(lambda h: h.tensor_scalar_mul(qs_t, qs_t, stat[:, 24:25]))
            tens(lambda h: h.transpose(pb[3][:, 0:128], qs_t, ident[:]), waits=[(tv, tv.v)])
            vec(lambda h: h.tensor_copy(qsf, pb[3][:, 0:128]), waits=[(tt, tt.v)])
            tens(lambda h: h.transpose(pb[3][:, 128:256], qtok, ident[:]), waits=[(tv, tv.v)])
            vec(lambda h: h.tensor_copy(qf, pb[3][:, 128:256]), waits=[(tt, tt.v)])
            vec(lambda h, _e=mslot(kvh, 3): h.tensor_tensor(qsf, qsf, _e, op=ALU.mult))
            tens(lambda h, _k2=mslot(kvh, 4): h.matmul(pb[4][:, 0:128], _k2, qsf,
                                                       start=True, stop=True),
                 waits=[(tv, tv.v)])
            vec(lambda h: h.tensor_tensor(pP, pb[4][:, 0:128], mgla[:], op=ALU.mult),
                waits=[(tt, tt.v)])
            tens(lambda h, _s=sgla[:, kvh * 128:(kvh + 1) * 128]:
                 h.matmul(pb[5][:, 0:128], _s, qsf, start=True, stop=False),
                 waits=[(tv, tv.v)])
            tens(lambda h, _v=vtok[:, kvh * 128:(kvh + 1) * 128]:
                 h.matmul(pb[5][:, 0:128], _v, pP, start=False, stop=True))
            vec(lambda h: h.tensor_copy(oglab, pb[5][:, 0:128]), waits=[(tt, tt.v)])
            sca(lambda h: h.activation(rotb[0:64, :], qf[64:128, :], AFT.Copy, scale=-1.0),
                waits=[(tv, tv.v)])
            sca(lambda h: h.activation(rotb[64:128, :], qf[0:64, :], AFT.Copy))
            vec(lambda h, _c=cosb[:, c * 128:(c + 1) * 128]:
                h.tensor_tensor(kc_, qf, _c, op=ALU.mult), waits=[(ts, ts.v)])
            vec(lambda h, _s=sinb[:, c * 128:(c + 1) * 128]:
                h.tensor_tensor(ks2_, rotb, _s, op=ALU.mult))
            vec(lambda h: h.tensor_tensor(sqf, kc_, ks2_, op=ALU.add))
            for g in range(2):
                gb = 2 * c + g
                if g == 0:
                    vec(lambda h, _r=rollk[:, kvh * WSZ:(kvh + 1) * WSZ]:
                        h.tensor_copy(swin[:, 0:64], _r))
                    vec(lambda h, _k=mslot(kvh, 6): h.tensor_copy(swin[:, 64:128], _k[:, 0:64]))
                    vec(lambda h, _r=rollv[:, kvh * 128:(kvh + 1) * 128]:
                        h.tensor_copy(svwin[0:64, :], _r))
                    vec(lambda h, _v=vtok[:, kvh * 128:(kvh + 1) * 128]:
                        h.tensor_copy(svwin[64:128, :], _v[0:64, :]))
                    kwin_ap, vwin_ap = swin, svwin
                else:
                    kwin_ap, vwin_ap = mslot(kvh, 6), vtok[:, kvh * 128:(kvh + 1) * 128]
                tens(lambda h, _q=sqf[:, g * 64:g * 64 + 64], _k=kwin_ap:
                     h.matmul(pb[6][0:64, 0:128], _q, _k, start=True, stop=True),
                     waits=[(tv, tv.v)])
                mk = mswa[:, 0:128] if gb == 0 else mswa[:, 128:256]
                vec(lambda h, _m=mk: h.scalar_tensor_tensor(sw[0:64, :], pb[6][0:64, 0:128],
                                                            SCALE_B, _m, op0=ALU.mult, op1=ALU.add),
                    waits=[(tt, tt.v)])
                vec(lambda h: h.reduce_max(stat[0:64, 25:26], sw[0:64, :], axis=AX.X, negate=True))
                sca(lambda h: h.activation(sw[0:64, :], sw[0:64, :], AFT.Exp,
                                           bias=stat[0:64, 25:26], accum_out=stat[0:64, 26:27]),
                    waits=[(tv, tv.v)])
                vec(lambda h: h.reciprocal(stat[0:64, 27:28], stat[0:64, 26:27]),
                    waits=[(ts, ts.v)])
                vec(lambda h: h.tensor_scalar_mul(sw[0:64, :], sw[0:64, :], stat[0:64, 27:28]))
                tens(lambda h: h.transpose(pb[6][:, 128:192], sw[0:64, :], ident[0:64, 0:64]),
                     waits=[(tv, tv.v)])
                vec(lambda h: h.tensor_copy(pswT[:, 0:64], pb[6][:, 128:192]),
                    waits=[(tt, tt.v)])
                tens(lambda h, _v=vwin_ap:
                     h.matmul(pb[7][:, 0:64], _v, pswT[:, 0:64], start=True, stop=True),
                     waits=[(tv, tv.v)])
                vec(lambda h, _g=g: h.tensor_tensor(tmpm[:, _g * 64:_g * 64 + 64],
                                                    oglab[:, _g * 64:_g * 64 + 64],
                                                    pb[7][:, 0:64], op=ALU.add),
                    waits=[(tt, tt.v)])
            sca(lambda h, _o=obf[:, qh * 128:(qh + 1) * 128].bitcast(BF16):
                h.activation(_o, tmpm, AFT.Copy, scale=0.5), waits=[(tv, tv.v), (sg, sg.v)])
        for kvh in range(NKVH):
            tens(lambda h, _k3=mslot(kvh, 5), _v=vtok[:, kvh * 128:(kvh + 1) * 128]:
                 h.matmul(pb[4][:, 128:256], _k3, _v, start=True, stop=True),
                 waits=[(ts, ts.v), (tv, tv.v)])
            vec(lambda h, _s=sgla[:, kvh * 128:(kvh + 1) * 128], _d=stat[:, 18 + kvh:19 + kvh]:
                h.scalar_tensor_tensor(_s, _s, _d, pb[4][:, 128:256], op0=ALU.mult, op1=ALU.add),
                waits=[(tt, tt.v)])
            vec(lambda h, _k=mslot(kvh, 6), _r=rollk[:, kvh * WSZ:(kvh + 1) * WSZ]:
                h.tensor_copy(_r, _k[:, 64:128]))
            vec(lambda h, _v=vtok[:, kvh * 128:(kvh + 1) * 128],
                _r=rollv[:, kvh * 128:(kvh + 1) * 128]:
                h.tensor_copy(_r, _v[64:128, :]))
        if dbg:
            dma("gpsimd", sg, dbgo["dstate"][c], sgla[:, 0:128], waits=[(tv, tv.v)])
            dma("gpsimd", sg, dbgo["dA"][c], mixw[:, 2 * 128:3 * 128], waits=[(tv, tv.v)])
            dma("gpsimd", sg, dbgo["dK3"][c], mixw[:, 5 * 128:6 * 128], waits=[(tv, tv.v)])
            dma("gpsimd", sg, dbgo["dDl"][c], stat[:, 18:19], waits=[(tv, tv.v)])
            P.wait("vector", sg)
            P.wait("scalar", sg)
        src_ap = bass.AP(obf, 0, [[NQH * 128, 128], [128, NQH], [1, 128]]).bitcast(BF16)
        dst_ap = bass.AP(cc3_in_t, c * 128, [[S, 128], [128 * S, NQH], [1, 128]])
        dma("gpsimd", sg, dst_ap, src_ap, waits=[(ts, ts.v)])
    P.barrier(bar)

    P.emit("gpsimd", lambda h: h.collective_compute(
        "AllGather", ALU.bypass, ins=[cc3_in], outs=[cc3_out], replica_groups=RG),
        waits=[(sg, sg.v)], inc=cc)
    cc3_done = cc.v
    P.wait("sync", cc, cc3_done)
    P.wait("scalar", cc, cc3_done)
    P.wait("vector", cc, cc3_done)
    P.wait("tensor", cc, cc3_done)
    P.barrier(bar)

    for tile in range(NTILE):
        for t in range(NCH):
            dma("sync", sd, x_t[:, t * D:(t + 1) * D],
                x2sp[tile * T + t * 128: tile * T + (t + 1) * 128, :],
                waits=[(tv, tv.v), (tt, tt.v), (ts, ts.v), (sd, sd.v), (sg, sg.v)])
        bsb = smb[:, 0:128]
        bsb2 = smb[:, 128:256]
        for m in range(32):
            si = 0 if m < 16 else 1
            rowoff = (m % 16) * 128
            for t in range(NCH):
                colA = tile * T + t * 128
                colB = HALF + tile * T + t * 128
                c0col, c1col = (colA, colB) if m < 16 else (colB, colA)
                dma("scalar", s2, bsb, cc3_out[rowoff:rowoff + 128, c0col:c0col + 128],
                    waits=[(tv, tv.v)])
                dma("scalar", s2, bsb2,
                    cc3_out[MIXD + rowoff:MIXD + rowoff + 128, c1col:c1col + 128])
                P.wait("vector", s2)
                vec(lambda h, _si=si, _d=xnf_stat(m, t), _s=bsb:
                    h.tensor_scalar_mul(_d, _s, sel[:, _si:_si + 1]))
                vec(lambda h, _si=si, _d=xnf_stat(m, t), _s=bsb2:
                    h.scalar_tensor_tensor(_d, _s, sel[:, 1 - _si:2 - _si], _d,
                                           op0=ALU.mult, op1=ALU.add))
        if dbg and tile == 0:
            dma("gpsimd", sg, dbgo["domA"], arA[:], waits=[(tv, tv.v)])
            P.wait("tensor", sg)
            P.wait("vector", sg)
        P.barrier(bar)

        def cons_wbo(dg, t, p):
            vec(lambda h, _p=p, _x=x_t[:, t * D + dg * 512: t * D + (dg + 1) * 512]:
                h.tensor_tensor(_x, _x, _p, op=ALU.add), waits=[(tt, tt.v)])
        proj_tokmajor(wbo, D, cons_wbo, xnf_stat)
        P.barrier(bar)
        if dbg:
            for t in range(NCH):
                dma("gpsimd", sg, dbgo["dx3"][tile * T + t * 128: tile * T + (t + 1) * 128, :],
                    x_t[:, t * D:(t + 1) * D], waits=[(tv, tv.v)])
        emit_ln()
        P.barrier(bar)
        emit_ffn()
        P.barrier(bar)
        for t in range(NCH):
            dma("sync", sd, yout[tile * T + t * 128: tile * T + (t + 1) * 128, :],
                x_t[:, t * D:(t + 1) * D], waits=[(tv, tv.v)])
        P.barrier(bar)

    if dbg:
        dma("sync", sd, dbgo["dqb"], cc2q_in)
        dma("sync", sd, dbgo["dkb"], cc2k_in)
        dma("sync", sd, dbgo["dvb"], cc2v_in)
        dma("sync", sd, dbgo["domix"], cc3_in)

    P.wait("sync", sd)
    P.wait("gpsimd", sg)
    P.barrier(bar)

    with nc.Block() as block:
        @block.sync
        def _(h):
            for fn in P.steps["sync"]:
                fn(h)

        @block.gpsimd
        def _(h):
            for fn in P.steps["gpsimd"]:
                fn(h)

        @block.vector
        def _(h):
            for fn in P.steps["vector"]:
                fn(h)

        @block.scalar
        def _(h):
            for fn in P.steps["scalar"]:
                fn(h)

        @block.tensor
        def _(h):
            for fn in P.steps["tensor"]:
                fn(h)

    stack.close()
    return nc


def _host_constants(parity):
    ident = np.eye(128, dtype=np.float32)
    qpos = parity * HALF + np.arange(HALF)
    maskA = np.full((8, 128, S), NEGINF, dtype=np.float32)
    for qi in range(8):
        gq = parity * HALF + qi * 128 + np.arange(128)[:, None]
        kk = np.arange(S)[None, :]
        maskA[qi] = np.where(kk <= gq, 0.0, NEGINF).astype(np.float32)
    a = np.arange(WSZ)[:, None]
    cidx = np.arange(2 * WSZ)[None, :]
    rel = a + WSZ - cidx
    base = (rel >= 0) & (rel <= WSZ)
    m_norm = np.where(base, 0.0, NEGINF).astype(np.float32)
    keypos_first = -WSZ + cidx
    m_first = np.where(base & (keypos_first >= 0), 0.0, NEGINF).astype(np.float32)
    maskSWA = np.stack([m_first, m_norm]).astype(np.float32)
    jj = np.arange(128)[:, None]
    ii = np.arange(128)[None, :]
    maskGLA = (jj <= ii).astype(np.float32)
    half = np.arange(HD // 2, dtype=np.float64)
    inv = 1.0 / (10000.0 ** (2.0 * half / HD))
    pos = np.arange(S, dtype=np.float64)
    fr = pos[None, :] * inv[:, None]
    fr2 = np.concatenate([fr, fr], axis=0)
    costab = np.cos(fr2).astype(np.float32)
    sintab = np.sin(fr2).astype(np.float32)
    selv = np.zeros((128, 2), dtype=np.float32)
    selv[:, parity] = 1.0
    return ident, maskA, maskSWA, maskGLA, costab, sintab, selv


def _perm_cols(w, parity, hd, nh):
    if parity == 0:
        return np.ascontiguousarray(w)
    idx = np.concatenate([np.arange(nh // 2, nh), np.arange(0, nh // 2)])
    cols = (idx[:, None] * hd + np.arange(hd)[None, :]).reshape(-1)
    return np.ascontiguousarray(w[:, cols])


def _perm_rows(w, parity, hd, nh):
    if parity == 0:
        return np.ascontiguousarray(w)
    idx = np.concatenate([np.arange(nh // 2, nh), np.arange(0, nh // 2)])
    rows = (idx[:, None] * hd + np.arange(hd)[None, :]).reshape(-1)
    return np.ascontiguousarray(w[rows, :])


_NC_CACHE = {}


def kernel(L0_attn_a_x, L0_attn_a_wq, L0_attn_a_wk, L0_attn_a_wv, L0_attn_a_wo,
           L0_attn_b_Wq, L0_attn_b_Wk, L0_attn_b_Wv, L0_attn_b_Wo,
           L0_ffn_route_w, L0_ffn_A, L0_ffn_Bw, _dbg=False):
    x = np.asarray(L0_attn_a_x, dtype=np.float32)
    f32 = lambda a: np.ascontiguousarray(np.asarray(a, dtype=np.float32))
    waq, wak, wav, wao = map(f32, (L0_attn_a_wq, L0_attn_a_wk, L0_attn_a_wv, L0_attn_a_wo))
    wbq0, wbk0, wbv0, wbo0 = map(f32, (L0_attn_b_Wq, L0_attn_b_Wk, L0_attn_b_Wv, L0_attn_b_Wo))
    route_w, A, Bw = map(f32, (L0_ffn_route_w, L0_ffn_A, L0_ffn_Bw))
    rwT = np.ascontiguousarray(route_w.T)
    a2T = np.ascontiguousarray(A.reshape(E * R, D).T)
    bw2 = np.ascontiguousarray(Bw.transpose(0, 2, 1).reshape(E * R, D))

    key = "dbg" if _dbg else "std"
    if key not in _NC_CACHE:
        _NC_CACHE[key] = build_program(dbg=_dbg)
    nc = _NC_CACHE[key]

    in_maps = []
    for c in range(NCORE):
        b, p = c // 2, c % 2
        ident, maskA, maskSWA, maskGLA, costab, sintab, selv = _host_constants(p)
        im = {
            "xin": np.ascontiguousarray(x[b, p * HALF:(p + 1) * HALF, :]),
            "waq": waq, "wak": wak, "wav": wav, "wao": wao,
            "wbq": _perm_cols(wbq0, p, HD, NH),
            "wbk": _perm_cols(wbk0, p, HD, NKV),
            "wbv": _perm_cols(wbv0, p, HD, NKV),
            "wbo": _perm_rows(wbo0, p, HD, NH),
            "rwT": rwT, "a2T": a2T, "bw2": bw2,
            "identF": ident, "maskAin": maskA, "maskSWAin": maskSWA,
            "maskGLAin": maskGLA, "costab": costab, "sintab": sintab,
            "selin": selv,
        }
        in_maps.append(im)

    res = run_bass_kernel_spmd(nc, in_maps, core_ids=list(range(NCORE)))
    out = np.empty((B, S, D), dtype=np.float32)
    for c in range(NCORE):
        b, p = c // 2, c % 2
        out[b, p * HALF:(p + 1) * HALF, :] = res.results[c]["yout"]
    if _dbg:
        return out, res
    return out
